# version 24
# baseline (speedup 1.0000x reference)
import numpy as np
import concourse.bass as bass
import concourse.mybir as mybir
from concourse.bass_utils import run_bass_kernel_spmd
from contextlib import ExitStack

F32 = mybir.dt.float32
BF16 = mybir.dt.bfloat16
U8 = mybir.dt.uint8
AF = mybir.ActivationFunctionType
ALU = mybir.AluOpType

NCORES = 8
T = 1024
DC = 32
D = 4096
KVD = 1024


class KB:
    ENGS = ("tensor", "vector", "scalar", "sync", "gpsimd")

    def __init__(self):
        self.prog = {e: [] for e in self.ENGS}
        self.sems = {}
        self.cnt = {}

    def add_sem(self, name, handle):
        self.sems[name] = handle
        self.cnt[name] = 0

    def wait(self, eng, tok):
        if tok is None:
            return
        name, val = tok
        if val <= 0:
            return
        h = self.sems[name]
        self.prog[eng].append(lambda e, h=h, v=val: e.wait_ge(h, v))

    def waits(self, eng, toks):
        best = {}
        for t in toks:
            if t is None:
                continue
            n, v = t
            if v > best.get(n, -1):
                best[n] = v
        for n, v in best.items():
            self.wait(eng, (n, v))

    def op(self, eng, f, sem=None, n=1):
        if sem is None:
            self.prog[eng].append(f)
            return None
        h = self.sems[sem]
        self.cnt[sem] += n
        v = self.cnt[sem]
        self.prog[eng].append(lambda e, f=f, h=h, n=n: f(e).then_inc(h, n))
        return (sem, v)


class DmaQueue:

    def __init__(self, kb, eng, semnames):
        self.kb, self.eng = kb, eng
        self.semnames = list(semnames)
        self.k = len(self.semnames)
        self.j = 0
        self.compl = []

    def dma(self, f, wars=()):
        if self.j >= self.k:
            self.kb.wait(self.eng, self.compl[self.j - self.k])
        self.kb.waits(self.eng, list(wars))
        tok = self.kb.op(self.eng, f, sem=self.semnames[self.j % self.k], n=16)
        self.compl.append(tok)
        self.j += 1
        return tok


class WStream:

    def __init__(self, kb, q, slot_aps, init_wars=None):
        self.kb, self.q = kb, q
        self.slots = list(slot_aps)
        self.war = list(init_wars) if init_wars else [None] * len(self.slots)
        self.j = 0

    def load(self, src_ap, split=2):
        s = self.j % len(self.slots)
        self.j += 1
        dst = self.slots[s]
        cols = dst.shape[1] // split
        toks = []
        for i in range(split):
            wars = [self.war[s]] if (i == 0 and self.war[s] is not None) else []
            d = dst[:, i * cols:(i + 1) * cols]
            sp = src_ap[:, i * cols:(i + 1) * cols]
            toks.append(self.q.dma(lambda e, d=d, sp=sp: e.dma_start(out=d, in_=sp), wars=wars))
        return s, dst, toks

    def consumed(self, s, tok):
        self.war[s] = tok


class Builder:
    def __init__(self, nc):
        self.nc = nc
        self.kb = KB()
        self.st = ExitStack()

    def alloc(self):
        nc, st, kb = self.nc, self.st, self.kb
        ap = lambda t: t.ap()
        self.xt = ap(nc.dram_tensor("xt", [D, T], F32, kind="ExternalInput"))
        self.ft = ap(nc.dram_tensor("ft", [512, D], BF16, kind="ExternalInput"))
        self.coefs = ap(nc.dram_tensor("coefs", [128, 480], F32, kind="ExternalInput"))
        self.masks = ap(nc.dram_tensor("masks", [128, 480], U8, kind="ExternalInput"))
        self.wq = ap(nc.dram_tensor("wq", [32, 128, 32, 128], F32, kind="ExternalInput"))
        self.wk = ap(nc.dram_tensor("wk", [2, 128, 32, 512], F32, kind="ExternalInput"))
        self.wv = ap(nc.dram_tensor("wv", [2, 128, 32, 512], F32, kind="ExternalInput"))
        self.wbo = ap(nc.dram_tensor("wbo", [32, 2, 128, 32, 128], F32, kind="ExternalInput"))
        self.enc = ap(nc.dram_tensor("enc", [128, 1280], F32, kind="ExternalInput"))
        self.bq = ap(nc.dram_tensor("bq", [128, 32], F32, kind="ExternalInput"))
        self.bo = ap(nc.dram_tensor("bo", [128, 32], F32, kind="ExternalInput"))
        self.bb = ap(nc.dram_tensor("bb", [128, 32], F32, kind="ExternalInput"))
        self.bkr = ap(nc.dram_tensor("bkr", [1, KVD], F32, kind="ExternalInput"))
        self.bvr = ap(nc.dram_tensor("bvr", [1, KVD], F32, kind="ExternalInput"))
        self.out = ap(nc.dram_tensor("out", [D, T], F32, kind="ExternalOutput"))
        self.cc1i = ap(nc.dram_tensor("cc1i", [128, 1032], F32))
        self.cc1o = ap(nc.dram_tensor("cc1o", [128, 1032], F32))
        self.cc2i = ap(nc.dram_tensor("cc2i", [40, D], F32))
        self.cc2o = ap(nc.dram_tensor("cc2o", [40, D], F32))

        sb = lambda n, s, d: self.st.enter_context(nc.sbuf_tensor(n, s, d))
        self.xn = sb("xn", [128, DC * T], BF16)
        self.qa = sb("qa", [128, DC * T], BF16)
        self.wring = sb("wring", [128, 3 * 4096], BF16)
        self.scr32 = sb("scr32", [128, 4096], F32)
        self.msc32 = sb("msc32", [128, 4096], F32)
        self.kvbf = sb("kvbf", [128, KVD], BF16)
        self.w01e = sb("w01e", [128, D], BF16)
        self.e01 = sb("e01", [128, T], BF16)
        self.encb = sb("encb", [128, 1280], BF16)
        self.bias = sb("bias", [128, 192], F32)
        self.ones_c = sb("ones_c", [128, 1], BF16)
        self.epsc = sb("epsc", [1, 4], F32)
        self.ones_r = sb("ones_r", [1, 128], F32)
        self.recip = sb("recip", [128, 16], F32)

        self.pAB = self.st.enter_context(nc.psum_tensor("pAB", [128, 2048], F32))
        self.pCD = self.st.enter_context(nc.psum_tensor("pCD", [128, 2048], F32))

        for name in ("pe", "dve", "act", "cc",
                     "w0", "w1", "w2", "w3", "w4",
                     "x0", "x1", "x2", "x3", "o0", "o1", "o2", "o3"):
            kb.add_sem(name, self.st.enter_context(nc.semaphore(name)))

        self.ws = DmaQueue(kb, "gpsimd", ("w0", "w1", "w2", "w3", "w4"))
        self.xs = DmaQueue(kb, "sync", ("x0", "x1", "x2", "x3"))
        self.os = DmaQueue(kb, "scalar", ("o0", "o1", "o2", "o3"))

        self.wring_slots = [self.wring[:, s * 4096:(s + 1) * 4096] for s in range(3)]
        self.wst = WStream(kb, self.ws, self.wring_slots)

        self.btok = [[] for _ in range(8)]

        self.rows = self.qa[:, 16384:24576].bitcast(F32)
        self.bkrow = self.qa[:, 26624:28672].bitcast(F32)
        self.bvrow = self.qa[:, 28672:30720].bitcast(F32)
        self.c01 = self.msc32[:, 0:480]
        self.m01 = self.msc32[:, 480:600].bitcast(U8)
        self.mf01 = self.msc32[:, 600:1080]
        self.prod = self.msc32[:, 1080:1560]
        self.w01s = self.msc32[:, 1560:1720]
        self.w01sb = self.msc32[:, 1720:1800].bitcast(BF16)

    def bank(self, b, lo=0, hi=512):
        t = self.pAB if b < 4 else self.pCD
        c = (b % 4) * 512
        return t[:, c + lo:c + hi]

    def reg2(self, r):
        t = self.pAB if r < 2 else self.pCD
        c = (r % 2) * 1024
        return t[:, c:c + 1024]

    def wait_banks(self, eng, banks):
        toks = []
        for b in banks:
            toks += self.btok[b]
        self.kb.waits(eng, toks)

    def set_banks(self, banks, toks):
        for b in banks:
            self.btok[b] = list(toks)

    def mm(self, out, lhsT, rhs, start, stop, sem=None):
        return self.kb.op(
            "tensor",
            lambda e, o=out, l=lhsT, r=rhs, st=start, sp=stop: e.matmul(o, l, r, start=st, stop=sp),
            sem=sem)

    def p0_setup(self):
        kb = self.kb
        self.ones_tok = kb.op("vector", lambda e: e.memset(self.ones_c[:], 1.0), sem="dve")
        self.onesr_tok = kb.op("vector", lambda e: e.memset(self.ones_r[:], 1.0), sem="dve")
        self.eps_tok = kb.op("vector", lambda e: e.memset(self.epsc[:], 1e-5), sem="dve")
    def p0_late(self):
        kb = self.kb
        self.ld_bq = self.xs.dma(lambda e: e.dma_start(out=self.bias[:, 0:32], in_=self.bq))
        self.ld_bo = self.xs.dma(lambda e: e.dma_start(out=self.bias[:, 64:96], in_=self.bo))
        self.ld_bb = self.xs.dma(lambda e: e.dma_start(out=self.bias[:, 96:128], in_=self.bb))
        self.ld_bkr = self.xs.dma(lambda e: e.dma_start(out=self.bkrow[0:1, :], in_=self.bkr))
        self.ld_bvr = self.xs.dma(lambda e: e.dma_start(out=self.bvrow[0:1, :], in_=self.bvr))
        kb.wait("vector", self.ld_bq)
        self.nbq_tok = kb.op("vector", lambda e: e.tensor_scalar_mul(
            self.bias[:, 32:64], self.bias[:, 0:32], -1.0), sem="dve")
        kb.waits("vector", [self.ld_bo, self.ld_bb])
        self.bc_tok = kb.op("vector", lambda e: e.tensor_tensor(
            self.bias[:, 128:160], self.bias[:, 64:96], self.bias[:, 96:128], ALU.add), sem="dve")

    def p4_w01(self):
        kb = self.kb
        self.ld_c01 = self.xs.dma(lambda e: e.dma_start(out=self.c01, in_=self.coefs))
        self.ld_m01 = self.xs.dma(lambda e: e.dma_start(out=self.m01, in_=self.masks))
        kb.wait("gpsimd", self.p1_pair_tok[3])
        ft_tok = []
        for j in range(4):
            dst = self.qa[:, j * 4096:(j + 1) * 4096]
            src = self.ft[j * 128:(j + 1) * 128, :]
            ft_tok.append(self.ws.dma(lambda e, d=dst, s=src: e.dma_start(out=d, in_=s)))
        kb.wait("vector", self.ld_m01)
        t_mf = kb.op("vector", lambda e: e.tensor_copy(self.mf01, self.m01), sem="dve")
        kb.wait("vector", self.ld_c01)
        kb.wait("vector", t_mf)
        t_pr = kb.op("vector", lambda e: e.tensor_tensor(self.prod, self.c01, self.mf01, ALU.mult), sem="dve")
        kb.wait("vector", t_pr)
        t_s1 = kb.op("vector", lambda e: e.tensor_tensor(
            self.w01s, self.prod[:, 0:160], self.prod[:, 160:320], ALU.add), sem="dve")
        kb.wait("vector", t_s1)
        t_s2 = kb.op("vector", lambda e: e.tensor_tensor(
            self.w01s, self.w01s, self.prod[:, 320:480], ALU.add), sem="dve")
        kb.wait("vector", t_s2)
        t_sb = kb.op("vector", lambda e: e.tensor_copy(self.w01sb, self.w01s), sem="dve")

        cc2_dma = []
        for half in range(2):
            self.wait_banks("tensor", [0, 1, 2, 3])
            kb.waits("tensor", ft_tok + [t_sb])
            tok = None
            for j in range(4):
                for ns in range(4):
                    lhsT = self.w01sb[:, j * 40:(j + 1) * 40]
                    rhs = self.qa[:, j * 4096 + half * 2048 + ns * 512:
                                  j * 4096 + half * 2048 + (ns + 1) * 512]
                    outp = self.pAB[0:40, ns * 512:(ns + 1) * 512]
                    tok = self.mm(outp, lhsT, rhs, start=(j == 0), stop=(j == 3),
                                  sem="pe" if (j == 3 and ns == 3) else None)
            piece_toks = []
            for p in range(2):
                slot = self.msc32[0:40, 2048 + p * 1024:2048 + (p + 1) * 1024]
                kb.wait("vector", tok)
                if half == 1:
                    kb.wait("vector", cc2_dma[p])
                et = kb.op("vector", lambda e, s=slot, p=p: e.tensor_copy(
                    s, self.pAB[0:40, p * 1024:(p + 1) * 1024]), sem="dve")
                piece_toks.append(et)
                dst = self.cc2i[0:40, half * 2048 + p * 1024: half * 2048 + (p + 1) * 1024]
                dtok = self.ws.dma(lambda e, d=dst, s=slot: e.dma_start(out=d, in_=s), wars=[et])
                if half == 0:
                    cc2_dma.append(dtok)
                else:
                    cc2_dma[p] = dtok
            self.set_banks([0, 1, 2, 3], piece_toks)
        kb.waits("gpsimd", cc2_dma)
        self.cc2_tok = kb.op("gpsimd", lambda e: e.collective_compute(
            "AllReduce", ALU.add, replica_groups=[[0, 1, 2, 3, 4, 5, 6, 7]],
            ins=[self.cc2i.opt()], outs=[self.cc2o.opt()]), sem="cc")
        self.cc2_dma = cc2_dma

    def p1_stats(self):
        kb = self.kb
        self.sq_tok = [None] * DC
        self.cast_tok = [None] * DC
        self.p1_pe = [None] * DC
        pair_tok = [None] * (DC // 2)

        def emit_pair_dma(j):
            base = (2 * j) % 4
            dslot = self.scr32[:, base * 1024:(base + 2) * 1024]
            wars = []
            if j >= 2:
                c4 = 2 * (j - 2)
                wars = [self.sq_tok[c4], self.cast_tok[c4],
                        self.sq_tok[c4 + 1], self.cast_tok[c4 + 1]]
            src = self.xt[2 * j * 128:(2 * j + 2) * 128, :].rearrange("(a p) n -> p a n", p=128)
            q = self.xs if j % 2 == 0 else self.os
            pair_tok[j] = q.dma(lambda e, d=dslot, s=src: e.dma_start(out=d, in_=s), wars=wars)

        def emit_chunk(c):
            slot = self.scr32[:, (c % 4) * 1024:((c % 4) + 1) * 1024]
            dtok = pair_tok[c // 2]
            xsq = self.qa[:, 24576 + (c % 2) * 1024: 24576 + ((c % 2) + 1) * 1024]
            xbf = self.xn[:, c * T:(c + 1) * T]
            kb.wait("vector", dtok)
            self.cast_tok[c] = kb.op("vector", lambda e, o=xbf, i=slot: e.tensor_copy(o, i), sem="dve")
            if c % 4 != 3:
                kb.wait("scalar", dtok)
                if c >= 2:
                    kb.wait("scalar", self.p1_pe[c - 2])
                self.sq_tok[c] = kb.op("scalar", lambda e, o=xsq, i=slot: e.activation(
                    o, i, AF.Square), sem="act")
            else:
                kb.wait("vector", self.cast_tok[c])
                if c >= 2:
                    kb.wait("vector", self.p1_pe[c - 2])
                self.sq_tok[c] = kb.op("vector", lambda e, o=xsq, i=xbf: e.tensor_tensor(
                    o, i, i, ALU.mult), sem="dve")
            kb.waits("tensor", [self.sq_tok[c], self.cast_tok[c], self.ones_tok])
            if c == 0:
                self.wait_banks("tensor", [4, 5, 6, 7])
            for th in range(2):
                self.mm(self.pCD[0:1, th * 512:(th + 1) * 512], self.ones_c[:],
                        xbf[:, th * 512:(th + 1) * 512], start=(c == 0), stop=(c == DC - 1))
            for th in range(2):
                self.p1_pe[c] = self.mm(
                    self.pCD[0:1, 1024 + th * 512:1024 + (th + 1) * 512], self.ones_c[:],
                    xsq[:, th * 512:(th + 1) * 512], start=(c == 0), stop=(c == DC - 1),
                    sem="pe" if th == 1 else None)

        for j in range(DC // 2 + 1):
            if j < DC // 2:
                emit_pair_dma(j)
            if j >= 1:
                emit_chunk(2 * (j - 1))
                emit_chunk(2 * (j - 1) + 1)
        self.p1_pair_tok = pair_tok

    def p1_rows(self):
        kb = self.kb
        m = self.rows[0:1, 0:1024]
        s0 = self.rows[0:1, 1024:2048]
        s1 = self.rows[0:1, 2048:3072]
        s2 = self.rows[0:1, 3072:4096]
        std = s1
        r = s0
        pe_end = self.p1_pe[DC - 1]
        kb.wait("scalar", pe_end)
        kb.wait("scalar", self.eps_tok)
        m_tok = kb.op("scalar", lambda e: e.mul(m, self.pCD[0:1, 0:1024], 1.0 / D), sem="act")
        ex2_tok = kb.op("scalar", lambda e: e.mul(s0, self.pCD[0:1, 1024:2048], 1.0 / D), sem="act")
        self.set_banks([4, 5], [m_tok])
        self.set_banks([6, 7], [ex2_tok])
        kb.wait("vector", ex2_tok)
        m2_tok = kb.op("vector", lambda e: e.tensor_tensor(s1, m, m, ALU.mult), sem="dve")
        kb.wait("vector", m2_tok)
        ve_tok = kb.op("vector", lambda e: e.tensor_tensor(s2, s0, s1, ALU.subtract), sem="dve")
        kb.wait("scalar", ve_tok)
        ln_tok = kb.op("scalar", lambda e: e.activation(
            s1, s2, AF.Ln, bias=self.epsc[0:1, 0:1]), sem="act")
        kb.wait("scalar", ln_tok)
        r_tok = kb.op("scalar", lambda e: e.activation(
            s0, s1, AF.Exp, scale=-0.5), sem="act")

        def bcast(row_ap, banks, dst, wait_toks):
            self.wait_banks("tensor", banks)
            kb.waits("tensor", wait_toks + [self.onesr_tok])
            tok = None
            t_ps = self.pAB if banks[0] < 4 else self.pCD
            base = (banks[0] % 4) * 512
            for th in range(2):
                tok = self.mm(t_ps[:, base + th * 512:base + (th + 1) * 512],
                              self.ones_r[:], row_ap[:, th * 512:(th + 1) * 512],
                              start=True, stop=True, sem="pe" if th == 1 else None)
            kb.wait("vector", tok)
            reg = t_ps[:, base:base + 1024]
            et = kb.op("vector", lambda e, d=dst, s=reg: e.tensor_copy(d, s), sem="dve")
            self.set_banks(banks, [et])
            return et

        self.mb_bf = self.qa[:, 31744:32768]
        self.rb_bf = self.qa[:, 30720:31744]
        self.mbbf_tok = bcast(m, [0, 1], self.mb_bf, [m_tok])
        self.rbbf_tok = bcast(r, [2, 3], self.rb_bf, [r_tok])
        self.bk_tok = bcast(self.bkrow[0:1, :], [0, 1], self.msc32[:, 0:1024], [self.ld_bkr])
        self.bv_tok = bcast(self.bvrow[0:1, :], [2, 3], self.msc32[:, 1024:2048], [self.ld_bvr])

    def p2_xn(self):
        kb = self.kb
        self.xn_tok = [None] * DC
        t_tok = [None] * DC
        for c in range(DC):
            tdst = self.qa[:, 24576 + (c % 2) * 1024:24576 + ((c % 2) + 1) * 1024]
            xslot = self.xn[:, c * T:(c + 1) * T]
            kb.waits("vector", [self.cast_tok[c], self.mbbf_tok])
            if c >= 2:
                kb.wait("vector", self.xn_tok[c - 2])
            t_tok[c] = kb.op("vector", lambda e, o=tdst, i=xslot: e.tensor_tensor(
                o, i, self.mb_bf, ALU.subtract), sem="dve")
            kb.waits("vector", [t_tok[c], self.rbbf_tok])
            self.xn_tok[c] = kb.op("vector", lambda e, o=xslot, i=tdst: e.tensor_tensor(
                o, i, self.rb_bf, ALU.mult), sem="dve")
        self.t_p2 = t_tok

    def p3_preload(self):
        src0 = self.wk[0][:, 0:8, :].rearrange("p a b -> p (a b)")
        src1 = self.wk[0][:, 8:16, :].rearrange("p a b -> p (a b)")
        self.wk_preload = [self.wst.load(src0), self.wst.load(src1)]

    def proj_wave(self, w_ap, ohalf, is_k):
        kb = self.kb

        def src(dcg):
            return w_ap[ohalf][:, dcg * 8:(dcg + 1) * 8, :].rearrange("p a b -> p (a b)")

        if is_k and ohalf == 0:
            blocks = list(self.wk_preload)
        else:
            blocks = [self.wst.load(src(0)), self.wst.load(src(1))]
        bank_tok = [None] * 8
        for dcg in range(4):
            slot, dst, toks = blocks[dcg]
            for dci in range(8):
                dc = dcg * 8 + dci
                kb.wait("tensor", toks[0] if dci < 4 else toks[1])
                if is_k and ohalf == 0:
                    kb.wait("tensor", self.xn_tok[dc])
                tc_order = (4, 5, 6, 7, 0, 1, 2, 3)
                for tc in tc_order:
                    if dc == 0:
                        self.wait_banks("tensor", [tc])
                    lhsT = self.xn[:, dc * T + tc * 128: dc * T + (tc + 1) * 128]
                    rhs = dst[:, dci * 512:(dci + 1) * 512]
                    last = (dc == DC - 1)
                    sem = "pe" if (last or (dci == 7 and tc == tc_order[-1])) else None
                    tok = self.mm(self.bank(tc), lhsT, rhs, start=(dc == 0), stop=last, sem=sem)
                    if last:
                        bank_tok[tc] = tok
            self.wst.consumed(slot, bank_tok[tc_order[-1]] if dcg == 3 else tok)
            if dcg < 2:
                blocks.append(self.wst.load(src(dcg + 2)))
        if is_k:
            z_tok = [None] * 8
            for tc in (4, 5, 6, 7, 0, 1, 2, 3):
                zs = self.zslot(tc)
                kb.waits("vector", [bank_tok[tc], self.bk_tok, self.zwar[tc]])
                z_tok[tc] = kb.op("vector", lambda e, o=zs, p=self.bank(tc): e.tensor_tensor(
                    o, p, self.msc32[:, ohalf * 512:ohalf * 512 + 512], ALU.add), sem="dve")
                self.set_banks([tc], [z_tok[tc]])
            w = self.kchain_war
            for tc in range(8):
                p = tc % 2
                zs = self.zslot(tc)
                t = self.msc32[:, 2048 + p * 512:2048 + (p + 1) * 512]
                ee = self.msc32[:, 3072 + p * 512:3072 + (p + 1) * 512]
                u = self.scr32[:, p * 512:(p + 1) * 512]
                kdst = self.qa[:, tc * 1024 + ohalf * 512: tc * 1024 + (ohalf + 1) * 512]
                kb.waits("scalar", [z_tok[tc]] + w["t"][p])
                t_tok = kb.op("scalar", lambda e, o=t, i=zs: e.activation(
                    o, i, AF.Relu, scale=-1.0), sem="act")
                kb.wait("scalar", t_tok)
                kb.waits("scalar", w["e"][p])
                e_tok = kb.op("scalar", lambda e, o=ee, i=t: e.activation(
                    o, i, AF.Exp, scale=-1.0), sem="act")
                kb.waits("vector", [t_tok] + w["u"][p])
                u_tok = kb.op("vector", lambda e, o=u, a=zs, b=t: e.tensor_tensor(
                    o, a, b, ALU.add), sem="dve")
                kb.waits("vector", [u_tok, e_tok])
                phi_tok = kb.op("vector", lambda e, o=kdst, a=u, b=ee: e.tensor_tensor(
                    o, a, b, ALU.add), sem="dve")
                w["t"][p] = [u_tok, e_tok]
                w["e"][p] = [phi_tok]
                w["u"][p] = [phi_tok]
                self.zwar[tc] = u_tok
                self.kphi_tok = phi_tok
        else:
            for tc in range(8):
                vdst = self.qa[:, 8192 + tc * 1024 + ohalf * 512: 8192 + tc * 1024 + (ohalf + 1) * 512]
                kb.waits("vector", [bank_tok[tc], self.bv_tok, self.zwar[tc]])
                vt = kb.op("vector", lambda e, o=vdst, p=self.bank(tc): e.tensor_tensor(
                    o, p, self.msc32[:, 1024 + ohalf * 512:1024 + ohalf * 512 + 512], ALU.add), sem="dve")
                self.set_banks([tc], [vt])
                self.v_tok.append(vt)

    def zslot(self, tc):
        return self.qa[:, 8192 + tc * 1024: 8192 + tc * 1024 + 1024].bitcast(F32)

    def p3_kv(self):
        kb = self.kb
        self.v_tok = []
        self.zwar = [self.xn_tok[DC - 1] for _ in range(8)]
        self.kchain_war = {"t": [list(self.cc2_dma), list(self.cc2_dma)],
                           "e": [list(self.cc2_dma), list(self.cc2_dma)],
                           "u": [[self.sq_tok[28], self.cast_tok[28]],
                                 [self.sq_tok[28], self.cast_tok[28]]]}
        self.proj_wave(self.wk, 0, True)
        self.proj_wave(self.wk, 1, True)
        self.proj_wave(self.wv, 0, False)
        self.proj_wave(self.wv, 1, False)
        self.p3_war = self.kchain_war
        self.wait_banks("tensor", [0, 1, 2])
        kb.waits("tensor", self.v_tok + [self.kphi_tok])
        kv_pe = None
        for kvh in range(8):
            for tc in range(8):
                k_ap = self.qa[:, tc * 1024 + kvh * 128: tc * 1024 + (kvh + 1) * 128]
                v_ap = self.qa[:, 8192 + tc * 1024 + kvh * 128: 8192 + tc * 1024 + (kvh + 1) * 128]
                self.mm(self.pAB[:, kvh * 128:(kvh + 1) * 128], k_ap, v_ap,
                        start=(tc == 0), stop=(tc == 7))
        for kvh in range(8):
            for tc in range(8):
                k_ap = self.qa[:, tc * 1024 + kvh * 128: tc * 1024 + (kvh + 1) * 128]
                kv_pe = self.mm(self.pAB[:, 1024 + kvh:1024 + kvh + 1], k_ap, self.ones_c[:],
                                start=(tc == 0), stop=(tc == 7),
                                sem="pe" if (tc == 7 and kvh == 7) else None)
        self.kv_pe = kv_pe
        self.enc_tok = self.ws.dma(lambda e: e.dma_start(out=self.encb[:], in_=self.enc))
        self.wait_banks("tensor", [4, 5])
        kb.waits("tensor", [self.enc_tok, self.xn_tok[DC - 1]])
        e01_pe = None
        for dc in range(DC):
            lhsT = self.encb[:, dc * 40:(dc + 1) * 40]
            for th in range(2):
                e01_pe = self.mm(self.pCD[0:40, th * 512:(th + 1) * 512], lhsT,
                                 self.xn[:, dc * T + th * 512: dc * T + (th + 1) * 512],
                                 start=(dc == 0), stop=(dc == DC - 1),
                                 sem="pe" if (dc == DC - 1 and th == 1) else None)
        kb.wait("vector", e01_pe)
        e01_tok = kb.op("vector", lambda e: e.tensor_copy(self.e01[0:40, :], self.pCD[0:40, 0:1024]), sem="dve")
        self.e01_tok = e01_tok
        self.set_banks([4, 5], [e01_tok])
        self.p5_loads = []
        for oc in range(3):
            src = self.wq[oc].rearrange("p a b -> p (a b)")
            self.p5_loads.append(self.wst.load(src, split=1))
        kb.wait("vector", kv_pe)
        kb.wait("vector", self.kphi_tok)
        kv_ev = kb.op("vector", lambda e: e.tensor_copy(
            self.msc32[:, 0:1024], self.pAB[:, 0:1024]), sem="dve")
        kb.wait("vector", kv_ev)
        nm_ev = kb.op("vector", lambda e: e.tensor_copy(
            self.msc32[:, 1024:1032], self.pAB[:, 1024:1032]), sem="dve")
        self.set_banks([0, 1], [kv_ev])
        self.set_banks([2], [nm_ev])
        dtok = self.ws.dma(lambda e: e.dma_start(out=self.cc1i[:, :], in_=self.msc32[:, 0:1032]),
                           wars=[kv_ev, nm_ev])
        kb.wait("gpsimd", dtok)
        self.cc1_tok = kb.op("gpsimd", lambda e: e.collective_compute(
            "AllReduce", ALU.add, replica_groups=[[0, 1, 2, 3], [4, 5, 6, 7]],
            ins=[self.cc1i.opt()], outs=[self.cc1o.opt()]), sem="cc")

    def p5_q(self):
        kb = self.kb
        loads = self.p5_loads
        p3w = self.p3_war
        war = {"t": p3w["t"][0] + p3w["t"][1], "e": p3w["e"][0] + p3w["e"][1],
               "u": p3w["u"][0] + p3w["u"][1], "phi": []}
        first = True
        for oc in range(DC):
            slot, dst, toks = loads[oc]
            r = (oc + 2) % 4
            self.wait_banks("tensor", [2 * r, 2 * r + 1])
            if first:
                kb.wait("tensor", self.xn_tok[DC - 1])
                first = False
            reg = self.reg2(r)
            pe_tok = None
            for dc in range(DC):
                if dc == 0:
                    kb.wait("tensor", toks[-1])
                lhsT = dst[:, dc * 128:(dc + 1) * 128]
                for th in range(2):
                    pe_tok = self.mm(reg[:, th * 512:(th + 1) * 512], lhsT,
                                     self.xn[:, dc * T + th * 512: dc * T + (th + 1) * 512],
                                     start=(dc == 0), stop=(dc == DC - 1),
                                     sem="pe" if (dc == DC - 1 and th == 1) else None)
            self.wst.consumed(slot, pe_tok)
            if oc + 3 < DC:
                src = self.wq[oc + 3].rearrange("p a b -> p (a b)")
                loads.append(self.wst.load(src, split=1))
            t = self.msc32[:, 2048:3072]
            ee = self.msc32[:, 3072:4096]
            u = self.scr32[:, 0:1024]
            qdst = self.qa[:, oc * T:(oc + 1) * T]
            nbq = self.bias[:, 32 + oc:33 + oc]
            bqc = self.bias[:, 0 + oc:1 + oc]
            kb.waits("scalar", [pe_tok, self.nbq_tok] + war["t"])
            t_tok = kb.op("scalar", lambda e, o=t, i=reg, b=nbq: e.activation(
                o, i, AF.Relu, bias=b, scale=-1.0), sem="act")
            kb.wait("scalar", t_tok)
            kb.waits("scalar", war["e"])
            e_tok = kb.op("scalar", lambda e, o=ee, i=t: e.activation(o, i, AF.Exp, scale=-1.0), sem="act")
            kb.waits("vector", [t_tok] + war["u"])
            u_tok = kb.op("vector", lambda e, o=u, i=t, b=bqc, p=reg: e.scalar_tensor_tensor(
                o, i, b, p, ALU.add, ALU.add), sem="dve")
            if oc == 0:
                kb.wait("vector", self.kv_pe)
            kb.waits("vector", [u_tok, e_tok] + war["phi"])
            phi_tok = kb.op("vector", lambda e, o=qdst, a=u, b=ee: e.tensor_tensor(
                o, a, b, ALU.add), sem="dve")
            war = {"t": [u_tok, e_tok], "e": [phi_tok], "u": [phi_tok], "phi": []}
            self.set_banks([2 * r, 2 * r + 1], [t_tok, u_tok])
            self.q_tok = phi_tok
            self.q_e_last = e_tok
            if oc == 16:
                self.emit_cc1_readback()

    def emit_cc1_readback(self):
        kb = self.kb
        kb.wait("sync", ("cc", 2))
        rb1 = self.xs.dma(lambda e: e.dma_start(out=self.msc32[:, 0:1032], in_=self.cc1o[:, :]),
                          wars=[self.bk_tok, self.kphi_tok])
        kb.wait("vector", rb1)
        self.kvc_tok = kb.op("vector", lambda e: e.tensor_copy(
            self.kvbf[:, :], self.msc32[:, 0:1024]), sem="dve")
        self.tmp_tok = kb.op("vector", lambda e: e.tensor_scalar_add(
            self.recip[:, 8:16], self.msc32[:, 1024:1032], 1e-10), sem="dve")
        kb.wait("vector", self.tmp_tok)
        self.rc_tok = kb.op("vector", lambda e: e.reciprocal(
            self.recip[:, 0:8], self.recip[:, 8:16]), sem="dve")

    def p6_attn(self):
        kb = self.kb
        kb.waits("tensor", [self.kvc_tok])
        self.attn_tok = None
        at_dve = at_act = None
        for qh in range(DC):
            kvh = qh // 4
            r = (qh + 2) % 4
            self.wait_banks("tensor", [2 * r, 2 * r + 1])
            reg = self.reg2(r)
            lhsT = self.kvbf[:, kvh * 128:(kvh + 1) * 128]
            pe_tok = None
            for th in range(2):
                pe_tok = self.mm(reg[:, th * 512:(th + 1) * 512], lhsT,
                                 self.qa[:, qh * T + th * 512: qh * T + (th + 1) * 512],
                                 start=True, stop=True, sem="pe" if th == 1 else None)
            odst = self.qa[:, qh * T:(qh + 1) * T]
            rcol = self.recip[:, kvh:kvh + 1]
            if qh % 2 == 0:
                kb.waits("vector", [pe_tok, self.rc_tok])
                ts_tok = kb.op("vector", lambda e, o=odst, p=reg, s=rcol:
                               e.tensor_scalar_mul(o, p, s), sem="dve")
                at_dve = ts_tok
            else:
                kb.waits("scalar", [pe_tok, self.rc_tok])
                ts_tok = kb.op("scalar", lambda e, o=odst, p=reg, s=rcol:
                               e.mul(o, p, s), sem="act")
                at_act = ts_tok
            self.set_banks([2 * r, 2 * r + 1], [ts_tok])
        self.attn_tok = at_dve
        self.attn_tok_act = at_act
        w01_cast = []
        for p in range(4):
            slot = self.scr32[0:40, 1024:2048]
            wars = [self.sq_tok[29], self.cast_tok[29]] if p == 0 else [w01_cast[p - 1]]
            dtok = self.xs.dma(lambda e, s=slot, i=self.cc2o[0:40, p * 1024:(p + 1) * 1024]:
                               e.dma_start(out=s, in_=i), wars=wars)
            kb.wait("vector", dtok)
            ct = kb.op("vector", lambda e, o=self.w01e[0:40, p * 1024:(p + 1) * 1024], s=slot:
                       e.tensor_scalar_mul(o, s, 0.5), sem="dve")
            w01_cast.append(ct)
        self.w01_tok = w01_cast[-1]
        self.w01_cast = w01_cast

    def p7_out(self):
        kb = self.kb
        slots = list(self.wring_slots) + [
            self.msc32[:, 0:2048].bitcast(BF16),
            self.msc32[:, 2048:4096].bitcast(BF16),
        ]
        init_wars = list(self.wst.war) + [self.tmp_tok, self.w01_tok]
        wst7 = WStream(kb, self.ws, slots, init_wars)
        loads = []
        for b in range(4):
            oc, w = divmod(b, 2)
            src = self.wbo[oc, w].rearrange("p a b -> p (a b)")
            loads.append(wst7.load(src, split=1))
        stt_tok = [None] * DC
        store_tok = [None] * DC
        for oc in range(DC):
            sb_, sbdst, sbtoks = loads[2 * oc]
            so_, sodst, sotoks = loads[2 * oc + 1]
            r = (oc + 2) % 4
            self.wait_banks("tensor", [2 * r, 2 * r + 1])
            reg = self.reg2(r)
            slot = self.scr32[:, (oc % 2) * 1024:((oc % 2) + 1) * 1024]
            if oc == 0:
                xwars = [self.q_tok]
            elif oc == 1:
                xwars = [self.w01_tok]
            else:
                xwars = [stt_tok[oc - 2]]
            src = self.xt[oc * 128:(oc + 1) * 128, :]
            xdtok = self.xs.dma(lambda e, d=slot, s=src: e.dma_start(out=d, in_=s), wars=xwars)
            pe_tok = None
            for dc in range(DC):
                if dc == 0:
                    kb.wait("tensor", sbtoks[-1])
                lhsT = sbdst[:, dc * 128:(dc + 1) * 128]
                for th in range(2):
                    self.mm(reg[:, th * 512:(th + 1) * 512], lhsT,
                            self.xn[:, dc * T + th * 512: dc * T + (th + 1) * 512],
                            start=(dc == 0), stop=False)
            for vc in range(DC):
                if vc == 0:
                    kb.wait("tensor", sotoks[-1])
                    if oc == 0:
                        kb.waits("tensor", [self.attn_tok, self.attn_tok_act])
                lhsT = sodst[:, vc * 128:(vc + 1) * 128]
                for th in range(2):
                    self.mm(reg[:, th * 512:(th + 1) * 512], lhsT,
                            self.qa[:, vc * T + th * 512: vc * T + (th + 1) * 512],
                            start=False, stop=False)
            if oc == 0:
                kb.waits("tensor", [self.e01_tok, self.w01_tok])
            for th in range(2):
                pe_tok = self.mm(reg[:, th * 512:(th + 1) * 512],
                                 self.w01e[0:40, oc * 128:(oc + 1) * 128],
                                 self.e01[0:40, th * 512:(th + 1) * 512],
                                 start=False, stop=True, sem="pe" if th == 1 else None)
            wst7.consumed(sb_, pe_tok)
            wst7.consumed(so_, pe_tok)
            nb = 2 * oc + 4
            while nb < 2 * DC and nb <= 2 * oc + 5:
                noc, w = divmod(nb, 2)
                src2 = self.wbo[noc, w].rearrange("p a b -> p (a b)")
                loads.append(wst7.load(src2, split=1))
                nb += 1
            stage = self.scr32[:, 2048 + (oc % 2) * 1024:2048 + ((oc % 2) + 1) * 1024]
            bc = self.bias[:, 128 + oc:129 + oc]
            kb.waits("vector", [pe_tok, xdtok, self.bc_tok])
            if oc >= 2:
                kb.wait("vector", store_tok[oc - 2])
            stt_tok[oc] = kb.op("vector", lambda e, o=stage, p=reg, b=bc, x=slot:
                                e.scalar_tensor_tensor(o, p, b, x, ALU.add, ALU.add), sem="dve")
            self.set_banks([2 * r, 2 * r + 1], [stt_tok[oc]])
            dst = self.out[oc * 128:(oc + 1) * 128, :]
            store_tok[oc] = self.os.dma(lambda e, d=dst, s=stage: e.dma_start(out=d, in_=s),
                                        wars=[stt_tok[oc]])
        for sname in ("o0", "o1", "o2", "o3"):
            kb.wait("scalar", (sname, self.kb.cnt[sname]))

    def build(self):
        self.alloc()
        self.p0_setup()
        self.p1_stats()
        self.p3_preload()
        self.p4_w01()
        self.p0_late()
        self.p1_rows()
        self.p2_xn()
        self.p3_kv()
        self.p5_q()
        self.p6_attn()
        self.p7_out()
        kb = self.kb
        with self.st:
            with self.nc.Block() as block:
                @block.tensor
                def _(eng):
                    for f in kb.prog["tensor"]:
                        f(eng)

                @block.vector
                def _(eng):
                    for f in kb.prog["vector"]:
                        f(eng)

                @block.scalar
                def _(eng):
                    for f in kb.prog["scalar"]:
                        f(eng)

                @block.sync
                def _(eng):
                    for f in kb.prog["sync"]:
                        f(eng)

                @block.gpsimd
                def _(eng):
                    for f in kb.prog["gpsimd"]:
                        f(eng)


def build_nc():
    nc = bass.Bass("TRN2", target_bir_lowering=False)
    Builder(nc).build()
    return nc


def _build_F():
    n = D
    t = np.arange(n, dtype=np.float64)[:, None]
    F = np.zeros((n, n), dtype=np.float64)
    F[:, 0] = 1.0 / np.sqrt(n)
    f = np.arange(1, n // 2, dtype=np.float64)[None, :]
    ang = 2.0 * np.pi * f * t / n
    F[:, 1:n - 1:2] = np.sqrt(2.0 / n) * np.cos(ang)
    F[:, 2:n - 1:2] = np.sqrt(2.0 / n) * np.sin(ang)
    F[:, n - 1] = ((-1.0) ** np.arange(n)) / np.sqrt(n)
    return F.astype(np.float32)


def _tile_w(WT, oc_cols):
    O = WT.shape[1]
    return np.ascontiguousarray(
        WT.reshape(32, 128, O // oc_cols, oc_cols).transpose(2, 1, 0, 3))


def kernel(L0_attn_hidden_states, L0_attn_Wq, L0_attn_bq, L0_attn_Wk, L0_attn_bk,
           L0_attn_Wv, L0_attn_bv, L0_attn_Wo, L0_attn_bo, L0_ffn_W_base,
           L0_ffn_b_base, L0_ffn_enc0, L0_ffn_enc1, L0_ffn_coefs0, L0_ffn_coefs1,
           L0_ffn_mask0, L0_ffn_mask1):
    x = np.asarray(L0_attn_hidden_states, np.float32).reshape(NCORES * T, D)

    wq_t = _tile_w(np.ascontiguousarray(np.asarray(L0_attn_Wq, np.float32).T), 128)
    wk_t = _tile_w(np.ascontiguousarray(np.asarray(L0_attn_Wk, np.float32).T), 512)
    wv_t = _tile_w(np.ascontiguousarray(np.asarray(L0_attn_Wv, np.float32).T), 512)
    wb_t = _tile_w(np.ascontiguousarray(np.asarray(L0_ffn_W_base, np.float32).T), 128)
    wo_t = _tile_w(np.ascontiguousarray(np.asarray(L0_attn_Wo, np.float32).T), 128)
    wbo_t = np.ascontiguousarray(np.stack([wb_t, wo_t], axis=1))

    import ml_dtypes
    F = _build_F()
    FT = np.ascontiguousarray(F.T.astype(ml_dtypes.bfloat16))

    enc = np.concatenate([np.asarray(L0_ffn_enc0, np.float32),
                          np.asarray(L0_ffn_enc1, np.float32)], axis=0)
    enc_t = np.ascontiguousarray(enc.T.reshape(32, 128, 40).transpose(1, 0, 2).reshape(128, 1280))

    c01 = np.concatenate([np.asarray(L0_ffn_coefs0, np.float32),
                          np.asarray(L0_ffn_coefs1, np.float32)], axis=2)
    m01 = np.concatenate([np.asarray(L0_ffn_mask0).view(np.uint8),
                          np.asarray(L0_ffn_mask1).view(np.uint8)], axis=2)
    c01_t = c01.reshape(3, 32, 128, 40)
    m01_t = m01.reshape(3, 32, 128, 40)

    bq_c = np.ascontiguousarray(np.asarray(L0_attn_bq, np.float32).reshape(32, 128).T)
    bo_c = np.ascontiguousarray(np.asarray(L0_attn_bo, np.float32).reshape(32, 128).T)
    bb_c = np.ascontiguousarray(np.asarray(L0_ffn_b_base, np.float32).reshape(32, 128).T)
    bk_r = np.ascontiguousarray(np.asarray(L0_attn_bk, np.float32).reshape(1, KVD))
    bv_r = np.ascontiguousarray(np.asarray(L0_attn_bv, np.float32).reshape(1, KVD))

    in_maps = []
    for c in range(NCORES):
        xt_c = np.ascontiguousarray(x[c * T:(c + 1) * T].T)
        ft_c = np.ascontiguousarray(FT[c * 512:(c + 1) * 512, :])
        cc = np.ascontiguousarray(c01_t[:, 4 * c:4 * c + 4].transpose(2, 0, 1, 3).reshape(128, 480))
        mm_ = np.ascontiguousarray(m01_t[:, 4 * c:4 * c + 4].transpose(2, 0, 1, 3).reshape(128, 480))
        in_maps.append({
            "xt": xt_c, "ft": ft_c, "coefs": cc, "masks": mm_,
            "wq": wq_t, "wk": wk_t, "wv": wv_t, "wbo": wbo_t,
            "enc": enc_t, "bq": bq_c, "bo": bo_c, "bb": bb_c,
            "bkr": bk_r, "bvr": bv_r,
        })

    nc = build_nc()
    res = run_bass_kernel_spmd(nc, in_maps, core_ids=list(range(NCORES)))
    outs = np.empty((NCORES * T, D), np.float32)
    for c in range(NCORES):
        outs[c * T:(c + 1) * T] = res.results[c]["out"].T
    return outs.reshape(2, T * 4, D)


# version 25
# speedup vs baseline: 1.0032x; 1.0032x over previous
import numpy as np
import concourse.bass as bass
import concourse.mybir as mybir
from concourse.bass_utils import run_bass_kernel_spmd
from contextlib import ExitStack

F32 = mybir.dt.float32
BF16 = mybir.dt.bfloat16
U8 = mybir.dt.uint8
AF = mybir.ActivationFunctionType
ALU = mybir.AluOpType

NCORES = 8
T = 1024
DC = 32
D = 4096
KVD = 1024


class KB:
    ENGS = ("tensor", "vector", "scalar", "sync", "gpsimd")

    def __init__(self):
        self.prog = {e: [] for e in self.ENGS}
        self.sems = {}
        self.cnt = {}

    def add_sem(self, name, handle):
        self.sems[name] = handle
        self.cnt[name] = 0

    def wait(self, eng, tok):
        if tok is None:
            return
        name, val = tok
        if val <= 0:
            return
        h = self.sems[name]
        self.prog[eng].append(lambda e, h=h, v=val: e.wait_ge(h, v))

    def waits(self, eng, toks):
        best = {}
        for t in toks:
            if t is None:
                continue
            n, v = t
            if v > best.get(n, -1):
                best[n] = v
        for n, v in best.items():
            self.wait(eng, (n, v))

    def op(self, eng, f, sem=None, n=1):
        if sem is None:
            self.prog[eng].append(f)
            return None
        h = self.sems[sem]
        self.cnt[sem] += n
        v = self.cnt[sem]
        self.prog[eng].append(lambda e, f=f, h=h, n=n: f(e).then_inc(h, n))
        return (sem, v)


class DmaQueue:

    def __init__(self, kb, eng, semnames):
        self.kb, self.eng = kb, eng
        self.semnames = list(semnames)
        self.k = len(self.semnames)
        self.j = 0
        self.compl = []

    def dma(self, f, wars=()):
        if self.j >= self.k:
            self.kb.wait(self.eng, self.compl[self.j - self.k])
        self.kb.waits(self.eng, list(wars))
        tok = self.kb.op(self.eng, f, sem=self.semnames[self.j % self.k], n=16)
        self.compl.append(tok)
        self.j += 1
        return tok


class WStream:

    def __init__(self, kb, q, slot_aps, init_wars=None):
        self.kb, self.q = kb, q
        self.slots = list(slot_aps)
        self.war = list(init_wars) if init_wars else [None] * len(self.slots)
        self.j = 0

    def load(self, src_ap, split=2):
        s = self.j % len(self.slots)
        self.j += 1
        dst = self.slots[s]
        cols = dst.shape[1] // split
        toks = []
        for i in range(split):
            wars = [self.war[s]] if (i == 0 and self.war[s] is not None) else []
            d = dst[:, i * cols:(i + 1) * cols]
            sp = src_ap[:, i * cols:(i + 1) * cols]
            toks.append(self.q.dma(lambda e, d=d, sp=sp: e.dma_start(out=d, in_=sp), wars=wars))
        return s, dst, toks

    def consumed(self, s, tok):
        self.war[s] = tok


class Builder:
    def __init__(self, nc):
        self.nc = nc
        self.kb = KB()
        self.st = ExitStack()

    def alloc(self):
        nc, st, kb = self.nc, self.st, self.kb
        ap = lambda t: t.ap()
        self.xt = ap(nc.dram_tensor("xt", [D, T], F32, kind="ExternalInput"))
        self.ft = ap(nc.dram_tensor("ft", [512, D], BF16, kind="ExternalInput"))
        self.coefs = ap(nc.dram_tensor("coefs", [128, 480], F32, kind="ExternalInput"))
        self.masks = ap(nc.dram_tensor("masks", [128, 480], U8, kind="ExternalInput"))
        self.wq = ap(nc.dram_tensor("wq", [32, 128, 32, 128], F32, kind="ExternalInput"))
        self.wk = ap(nc.dram_tensor("wk", [2, 128, 32, 512], F32, kind="ExternalInput"))
        self.wv = ap(nc.dram_tensor("wv", [2, 128, 32, 512], F32, kind="ExternalInput"))
        self.wbo = ap(nc.dram_tensor("wbo", [32, 2, 128, 32, 128], F32, kind="ExternalInput"))
        self.enc = ap(nc.dram_tensor("enc", [128, 1280], F32, kind="ExternalInput"))
        self.bq = ap(nc.dram_tensor("bq", [128, 32], F32, kind="ExternalInput"))
        self.bo = ap(nc.dram_tensor("bo", [128, 32], F32, kind="ExternalInput"))
        self.bb = ap(nc.dram_tensor("bb", [128, 32], F32, kind="ExternalInput"))
        self.bkr = ap(nc.dram_tensor("bkr", [1, KVD], F32, kind="ExternalInput"))
        self.bvr = ap(nc.dram_tensor("bvr", [1, KVD], F32, kind="ExternalInput"))
        self.out = ap(nc.dram_tensor("out", [D, T], F32, kind="ExternalOutput"))
        self.cc1i = ap(nc.dram_tensor("cc1i", [128, 1032], F32))
        self.cc1o = ap(nc.dram_tensor("cc1o", [128, 1032], F32))
        self.cc2i = ap(nc.dram_tensor("cc2i", [40, D], F32))
        self.cc2o = ap(nc.dram_tensor("cc2o", [40, D], F32))

        sb = lambda n, s, d: self.st.enter_context(nc.sbuf_tensor(n, s, d))
        self.xn = sb("xn", [128, DC * T], BF16)
        self.qa = sb("qa", [128, DC * T], BF16)
        self.wring = sb("wring", [128, 3 * 4096], BF16)
        self.scr32 = sb("scr32", [128, 4096], F32)
        self.msc32 = sb("msc32", [128, 4096], F32)
        self.kvbf = sb("kvbf", [128, KVD], BF16)
        self.w01e = sb("w01e", [128, D], BF16)
        self.e01 = sb("e01", [128, T], BF16)
        self.encb = sb("encb", [128, 1280], BF16)
        self.bias = sb("bias", [128, 192], F32)
        self.ones_c = sb("ones_c", [128, 1], BF16)
        self.epsc = sb("epsc", [1, 4], F32)
        self.ones_r = sb("ones_r", [1, 128], F32)
        self.recip = sb("recip", [128, 16], F32)

        self.pAB = self.st.enter_context(nc.psum_tensor("pAB", [128, 2048], F32))
        self.pCD = self.st.enter_context(nc.psum_tensor("pCD", [128, 2048], F32))

        for name in ("pe", "dve", "act", "cc",
                     "w0", "w1", "w2", "w3", "w4",
                     "x0", "x1", "x2", "x3", "o0", "o1", "o2", "o3"):
            kb.add_sem(name, self.st.enter_context(nc.semaphore(name)))

        self.ws = DmaQueue(kb, "gpsimd", ("w0", "w1", "w2", "w3", "w4"))
        self.xs = DmaQueue(kb, "sync", ("x0", "x1", "x2", "x3"))
        self.os = DmaQueue(kb, "scalar", ("o0", "o1", "o2", "o3"))

        self.wring_slots = [self.wring[:, s * 4096:(s + 1) * 4096] for s in range(3)]
        self.wst = WStream(kb, self.ws, self.wring_slots)

        self.btok = [[] for _ in range(8)]

        self.rows = self.qa[:, 16384:24576].bitcast(F32)
        self.bkrow = self.qa[:, 26624:28672].bitcast(F32)
        self.bvrow = self.qa[:, 28672:30720].bitcast(F32)
        self.c01 = self.msc32[:, 0:480]
        self.m01 = self.msc32[:, 480:600].bitcast(U8)
        self.mf01 = self.msc32[:, 600:1080]
        self.prod = self.msc32[:, 1080:1560]
        self.w01s = self.msc32[:, 1560:1720]
        self.w01sb = self.msc32[:, 1720:1800].bitcast(BF16)

    def bank(self, b, lo=0, hi=512):
        t = self.pAB if b < 4 else self.pCD
        c = (b % 4) * 512
        return t[:, c + lo:c + hi]

    def reg2(self, r):
        t = self.pAB if r < 2 else self.pCD
        c = (r % 2) * 1024
        return t[:, c:c + 1024]

    def wait_banks(self, eng, banks):
        toks = []
        for b in banks:
            toks += self.btok[b]
        self.kb.waits(eng, toks)

    def set_banks(self, banks, toks):
        for b in banks:
            self.btok[b] = list(toks)

    def mm(self, out, lhsT, rhs, start, stop, sem=None):
        return self.kb.op(
            "tensor",
            lambda e, o=out, l=lhsT, r=rhs, st=start, sp=stop: e.matmul(o, l, r, start=st, stop=sp),
            sem=sem)

    def p0_setup(self):
        kb = self.kb
        self.ones_tok = kb.op("vector", lambda e: e.memset(self.ones_c[:], 1.0), sem="dve")
        self.onesr_tok = kb.op("vector", lambda e: e.memset(self.ones_r[:], 1.0), sem="dve")
        self.eps_tok = kb.op("vector", lambda e: e.memset(self.epsc[:], 1e-5), sem="dve")
    def p0_late(self):
        kb = self.kb
        self.ld_bq = self.xs.dma(lambda e: e.dma_start(out=self.bias[:, 0:32], in_=self.bq))
        self.ld_bo = self.xs.dma(lambda e: e.dma_start(out=self.bias[:, 64:96], in_=self.bo))
        self.ld_bb = self.xs.dma(lambda e: e.dma_start(out=self.bias[:, 96:128], in_=self.bb))
        self.ld_bkr = self.xs.dma(lambda e: e.dma_start(out=self.bkrow[0:1, :], in_=self.bkr))
        self.ld_bvr = self.xs.dma(lambda e: e.dma_start(out=self.bvrow[0:1, :], in_=self.bvr))
        kb.wait("vector", self.ld_bq)
        self.nbq_tok = kb.op("vector", lambda e: e.tensor_scalar_mul(
            self.bias[:, 32:64], self.bias[:, 0:32], -1.0), sem="dve")
        kb.waits("vector", [self.ld_bo, self.ld_bb])
        self.bc_tok = kb.op("vector", lambda e: e.tensor_tensor(
            self.bias[:, 128:160], self.bias[:, 64:96], self.bias[:, 96:128], ALU.add), sem="dve")

    def p4_w01(self):
        kb = self.kb
        self.ld_c01 = self.xs.dma(lambda e: e.dma_start(out=self.c01, in_=self.coefs))
        self.ld_m01 = self.xs.dma(lambda e: e.dma_start(out=self.m01, in_=self.masks))
        ft_tok = []
        for j in range(4):
            dst = self.qa[:, j * 4096:(j + 1) * 4096]
            src = self.ft[j * 128:(j + 1) * 128, :]
            ft_tok.append(self.ws.dma(lambda e, d=dst, s=src: e.dma_start(out=d, in_=s)))
        kb.wait("vector", self.ld_m01)
        t_mf = kb.op("vector", lambda e: e.tensor_copy(self.mf01, self.m01), sem="dve")
        kb.wait("vector", self.ld_c01)
        kb.wait("vector", t_mf)
        t_pr = kb.op("vector", lambda e: e.tensor_tensor(self.prod, self.c01, self.mf01, ALU.mult), sem="dve")
        kb.wait("vector", t_pr)
        t_s1 = kb.op("vector", lambda e: e.tensor_tensor(
            self.w01s, self.prod[:, 0:160], self.prod[:, 160:320], ALU.add), sem="dve")
        kb.wait("vector", t_s1)
        t_s2 = kb.op("vector", lambda e: e.tensor_tensor(
            self.w01s, self.w01s, self.prod[:, 320:480], ALU.add), sem="dve")
        kb.wait("vector", t_s2)
        t_sb = kb.op("vector", lambda e: e.tensor_copy(self.w01sb, self.w01s), sem="dve")

        cc2_dma = []
        for half in range(2):
            self.wait_banks("tensor", [0, 1, 2, 3])
            kb.waits("tensor", ft_tok + [t_sb])
            tok = None
            for j in range(4):
                for ns in range(4):
                    lhsT = self.w01sb[:, j * 40:(j + 1) * 40]
                    rhs = self.qa[:, j * 4096 + half * 2048 + ns * 512:
                                  j * 4096 + half * 2048 + (ns + 1) * 512]
                    outp = self.pAB[0:40, ns * 512:(ns + 1) * 512]
                    tok = self.mm(outp, lhsT, rhs, start=(j == 0), stop=(j == 3),
                                  sem="pe" if (j == 3 and ns == 3) else None)
            piece_toks = []
            for p in range(2):
                slot = self.msc32[0:40, 2048 + p * 1024:2048 + (p + 1) * 1024]
                kb.wait("vector", tok)
                if half == 1:
                    kb.wait("vector", cc2_dma[p])
                et = kb.op("vector", lambda e, s=slot, p=p: e.tensor_copy(
                    s, self.pAB[0:40, p * 1024:(p + 1) * 1024]), sem="dve")
                piece_toks.append(et)
                dst = self.cc2i[0:40, half * 2048 + p * 1024: half * 2048 + (p + 1) * 1024]
                dtok = self.ws.dma(lambda e, d=dst, s=slot: e.dma_start(out=d, in_=s), wars=[et])
                if half == 0:
                    cc2_dma.append(dtok)
                else:
                    cc2_dma[p] = dtok
            self.set_banks([0, 1, 2, 3], piece_toks)
        kb.waits("gpsimd", cc2_dma)
        self.cc2_tok = kb.op("gpsimd", lambda e: e.collective_compute(
            "AllReduce", ALU.add, replica_groups=[[0, 1, 2, 3, 4, 5, 6, 7]],
            ins=[self.cc2i.opt()], outs=[self.cc2o.opt()]), sem="cc")
        self.cc2_dma = cc2_dma

    def p1_stats(self):
        kb = self.kb
        self.sq_tok = [None] * DC
        self.cast_tok = [None] * DC
        self.p1_pe = [None] * DC
        pair_tok = [None] * (DC // 2)

        def emit_pair_dma(j):
            base = (2 * j) % 8
            if base < 4:
                dslot = self.scr32[:, base * 1024:(base + 2) * 1024]
            else:
                dslot = self.rows[:, (base - 4) * 1024:(base - 2) * 1024]
            wars = []
            if j >= 4:
                c4 = 2 * (j - 4)
                wars = [self.sq_tok[c4], self.cast_tok[c4],
                        self.sq_tok[c4 + 1], self.cast_tok[c4 + 1]]
            src = self.xt[2 * j * 128:(2 * j + 2) * 128, :].rearrange("(a p) n -> p a n", p=128)
            q = self.xs if j % 2 == 0 else self.os
            pair_tok[j] = q.dma(lambda e, d=dslot, s=src: e.dma_start(out=d, in_=s), wars=wars)

        def emit_chunk(c):
            s = c % 8
            slot = (self.scr32[:, s * 1024:(s + 1) * 1024] if s < 4
                    else self.rows[:, (s - 4) * 1024:(s - 3) * 1024])
            dtok = pair_tok[c // 2]
            xsq = self.qa[:, 24576 + (c % 2) * 1024: 24576 + ((c % 2) + 1) * 1024]
            xbf = self.xn[:, c * T:(c + 1) * T]
            kb.wait("vector", dtok)
            self.cast_tok[c] = kb.op("vector", lambda e, o=xbf, i=slot: e.tensor_copy(o, i), sem="dve")
            if c % 4 != 3:
                kb.wait("scalar", dtok)
                if c >= 2:
                    kb.wait("scalar", self.p1_pe[c - 2])
                self.sq_tok[c] = kb.op("scalar", lambda e, o=xsq, i=slot: e.activation(
                    o, i, AF.Square), sem="act")
            else:
                kb.wait("vector", self.cast_tok[c])
                if c >= 2:
                    kb.wait("vector", self.p1_pe[c - 2])
                self.sq_tok[c] = kb.op("vector", lambda e, o=xsq, i=xbf: e.tensor_tensor(
                    o, i, i, ALU.mult), sem="dve")
            kb.waits("tensor", [self.sq_tok[c], self.cast_tok[c], self.ones_tok])
            if c == 0:
                self.wait_banks("tensor", [4, 5, 6, 7])
            for th in range(2):
                self.mm(self.pCD[0:1, th * 512:(th + 1) * 512], self.ones_c[:],
                        xbf[:, th * 512:(th + 1) * 512], start=(c == 0), stop=(c == DC - 1))
            for th in range(2):
                self.p1_pe[c] = self.mm(
                    self.pCD[0:1, 1024 + th * 512:1024 + (th + 1) * 512], self.ones_c[:],
                    xsq[:, th * 512:(th + 1) * 512], start=(c == 0), stop=(c == DC - 1),
                    sem="pe" if th == 1 else None)

        for j in range(DC // 2 + 1):
            if j < DC // 2:
                emit_pair_dma(j)
            if j >= 1:
                emit_chunk(2 * (j - 1))
                emit_chunk(2 * (j - 1) + 1)

    def p1_rows(self):
        kb = self.kb
        m = self.rows[0:1, 0:1024]
        s0 = self.rows[0:1, 1024:2048]
        s1 = self.rows[0:1, 2048:3072]
        s2 = self.rows[0:1, 3072:4096]
        std = s1
        r = s0
        pe_end = self.p1_pe[DC - 1]
        kb.wait("scalar", pe_end)
        kb.wait("scalar", self.eps_tok)
        m_tok = kb.op("scalar", lambda e: e.mul(m, self.pCD[0:1, 0:1024], 1.0 / D), sem="act")
        ex2_tok = kb.op("scalar", lambda e: e.mul(s0, self.pCD[0:1, 1024:2048], 1.0 / D), sem="act")
        self.set_banks([4, 5], [m_tok])
        self.set_banks([6, 7], [ex2_tok])
        kb.wait("vector", ex2_tok)
        m2_tok = kb.op("vector", lambda e: e.tensor_tensor(s1, m, m, ALU.mult), sem="dve")
        kb.wait("vector", m2_tok)
        ve_tok = kb.op("vector", lambda e: e.tensor_tensor(s2, s0, s1, ALU.subtract), sem="dve")
        kb.wait("scalar", ve_tok)
        ln_tok = kb.op("scalar", lambda e: e.activation(
            s1, s2, AF.Ln, bias=self.epsc[0:1, 0:1]), sem="act")
        kb.wait("scalar", ln_tok)
        r_tok = kb.op("scalar", lambda e: e.activation(
            s0, s1, AF.Exp, scale=-0.5), sem="act")

        def bcast(row_ap, banks, dst, wait_toks):
            self.wait_banks("tensor", banks)
            kb.waits("tensor", wait_toks + [self.onesr_tok])
            tok = None
            t_ps = self.pAB if banks[0] < 4 else self.pCD
            base = (banks[0] % 4) * 512
            for th in range(2):
                tok = self.mm(t_ps[:, base + th * 512:base + (th + 1) * 512],
                              self.ones_r[:], row_ap[:, th * 512:(th + 1) * 512],
                              start=True, stop=True, sem="pe" if th == 1 else None)
            kb.wait("vector", tok)
            reg = t_ps[:, base:base + 1024]
            et = kb.op("vector", lambda e, d=dst, s=reg: e.tensor_copy(d, s), sem="dve")
            self.set_banks(banks, [et])
            return et

        self.mb_bf = self.qa[:, 31744:32768]
        self.rb_bf = self.qa[:, 30720:31744]
        self.mbbf_tok = bcast(m, [0, 1], self.mb_bf, [m_tok])
        self.rbbf_tok = bcast(r, [2, 3], self.rb_bf, [r_tok])
        self.bk_tok = bcast(self.bkrow[0:1, :], [0, 1], self.msc32[:, 0:1024], [self.ld_bkr])
        self.bv_tok = bcast(self.bvrow[0:1, :], [2, 3], self.msc32[:, 1024:2048], [self.ld_bvr])

    def p2_xn(self):
        kb = self.kb
        self.xn_tok = [None] * DC
        t_tok = [None] * DC
        for c in range(DC):
            tdst = self.qa[:, 24576 + (c % 2) * 1024:24576 + ((c % 2) + 1) * 1024]
            xslot = self.xn[:, c * T:(c + 1) * T]
            kb.waits("vector", [self.cast_tok[c], self.mbbf_tok])
            if c >= 2:
                kb.wait("vector", self.xn_tok[c - 2])
            t_tok[c] = kb.op("vector", lambda e, o=tdst, i=xslot: e.tensor_tensor(
                o, i, self.mb_bf, ALU.subtract), sem="dve")
            kb.waits("vector", [t_tok[c], self.rbbf_tok])
            self.xn_tok[c] = kb.op("vector", lambda e, o=xslot, i=tdst: e.tensor_tensor(
                o, i, self.rb_bf, ALU.mult), sem="dve")
        self.t_p2 = t_tok

    def p3_preload(self):
        src0 = self.wk[0][:, 0:8, :].rearrange("p a b -> p (a b)")
        src1 = self.wk[0][:, 8:16, :].rearrange("p a b -> p (a b)")
        self.wk_preload = [self.wst.load(src0), self.wst.load(src1)]

    def proj_wave(self, w_ap, ohalf, is_k):
        kb = self.kb

        def src(dcg):
            return w_ap[ohalf][:, dcg * 8:(dcg + 1) * 8, :].rearrange("p a b -> p (a b)")

        if is_k and ohalf == 0:
            blocks = list(self.wk_preload)
        else:
            blocks = [self.wst.load(src(0)), self.wst.load(src(1))]
        bank_tok = [None] * 8
        for dcg in range(4):
            slot, dst, toks = blocks[dcg]
            for dci in range(8):
                dc = dcg * 8 + dci
                kb.wait("tensor", toks[0] if dci < 4 else toks[1])
                if is_k and ohalf == 0:
                    kb.wait("tensor", self.xn_tok[dc])
                tc_order = (4, 5, 6, 7, 0, 1, 2, 3)
                for tc in tc_order:
                    if dc == 0:
                        self.wait_banks("tensor", [tc])
                    lhsT = self.xn[:, dc * T + tc * 128: dc * T + (tc + 1) * 128]
                    rhs = dst[:, dci * 512:(dci + 1) * 512]
                    last = (dc == DC - 1)
                    sem = "pe" if (last or (dci == 7 and tc == tc_order[-1])) else None
                    tok = self.mm(self.bank(tc), lhsT, rhs, start=(dc == 0), stop=last, sem=sem)
                    if last:
                        bank_tok[tc] = tok
            self.wst.consumed(slot, bank_tok[tc_order[-1]] if dcg == 3 else tok)
            if dcg < 2:
                blocks.append(self.wst.load(src(dcg + 2)))
        if is_k:
            z_tok = [None] * 8
            for tc in (4, 5, 6, 7, 0, 1, 2, 3):
                zs = self.zslot(tc)
                kb.waits("vector", [bank_tok[tc], self.bk_tok, self.zwar[tc]])
                z_tok[tc] = kb.op("vector", lambda e, o=zs, p=self.bank(tc): e.tensor_tensor(
                    o, p, self.msc32[:, ohalf * 512:ohalf * 512 + 512], ALU.add), sem="dve")
                self.set_banks([tc], [z_tok[tc]])
            w = self.kchain_war
            for tc in range(8):
                p = tc % 2
                zs = self.zslot(tc)
                t = self.msc32[:, 2048 + p * 512:2048 + (p + 1) * 512]
                ee = self.msc32[:, 3072 + p * 512:3072 + (p + 1) * 512]
                u = self.scr32[:, p * 512:(p + 1) * 512]
                kdst = self.qa[:, tc * 1024 + ohalf * 512: tc * 1024 + (ohalf + 1) * 512]
                kb.waits("scalar", [z_tok[tc]] + w["t"][p])
                t_tok = kb.op("scalar", lambda e, o=t, i=zs: e.activation(
                    o, i, AF.Relu, scale=-1.0), sem="act")
                kb.wait("scalar", t_tok)
                kb.waits("scalar", w["e"][p])
                e_tok = kb.op("scalar", lambda e, o=ee, i=t: e.activation(
                    o, i, AF.Exp, scale=-1.0), sem="act")
                kb.waits("vector", [t_tok] + w["u"][p])
                u_tok = kb.op("vector", lambda e, o=u, a=zs, b=t: e.tensor_tensor(
                    o, a, b, ALU.add), sem="dve")
                kb.waits("vector", [u_tok, e_tok])
                phi_tok = kb.op("vector", lambda e, o=kdst, a=u, b=ee: e.tensor_tensor(
                    o, a, b, ALU.add), sem="dve")
                w["t"][p] = [u_tok, e_tok]
                w["e"][p] = [phi_tok]
                w["u"][p] = [phi_tok]
                self.zwar[tc] = u_tok
                self.kphi_tok = phi_tok
        else:
            for tc in range(8):
                vdst = self.qa[:, 8192 + tc * 1024 + ohalf * 512: 8192 + tc * 1024 + (ohalf + 1) * 512]
                kb.waits("vector", [bank_tok[tc], self.bv_tok, self.zwar[tc]])
                vt = kb.op("vector", lambda e, o=vdst, p=self.bank(tc): e.tensor_tensor(
                    o, p, self.msc32[:, 1024 + ohalf * 512:1024 + ohalf * 512 + 512], ALU.add), sem="dve")
                self.set_banks([tc], [vt])
                self.v_tok.append(vt)

    def zslot(self, tc):
        return self.qa[:, 8192 + tc * 1024: 8192 + tc * 1024 + 1024].bitcast(F32)

    def p3_kv(self):
        kb = self.kb
        self.v_tok = []
        self.zwar = [self.xn_tok[DC - 1] for _ in range(8)]
        self.kchain_war = {"t": [list(self.cc2_dma), list(self.cc2_dma)],
                           "e": [list(self.cc2_dma), list(self.cc2_dma)],
                           "u": [[self.sq_tok[24], self.cast_tok[24]],
                                 [self.sq_tok[24], self.cast_tok[24]]]}
        self.proj_wave(self.wk, 0, True)
        self.proj_wave(self.wk, 1, True)
        self.proj_wave(self.wv, 0, False)
        self.proj_wave(self.wv, 1, False)
        self.p3_war = self.kchain_war
        self.wait_banks("tensor", [0, 1, 2])
        kb.waits("tensor", self.v_tok + [self.kphi_tok])
        kv_pe = None
        for kvh in range(8):
            for tc in range(8):
                k_ap = self.qa[:, tc * 1024 + kvh * 128: tc * 1024 + (kvh + 1) * 128]
                v_ap = self.qa[:, 8192 + tc * 1024 + kvh * 128: 8192 + tc * 1024 + (kvh + 1) * 128]
                self.mm(self.pAB[:, kvh * 128:(kvh + 1) * 128], k_ap, v_ap,
                        start=(tc == 0), stop=(tc == 7))
        for kvh in range(8):
            for tc in range(8):
                k_ap = self.qa[:, tc * 1024 + kvh * 128: tc * 1024 + (kvh + 1) * 128]
                kv_pe = self.mm(self.pAB[:, 1024 + kvh:1024 + kvh + 1], k_ap, self.ones_c[:],
                                start=(tc == 0), stop=(tc == 7),
                                sem="pe" if (tc == 7 and kvh == 7) else None)
        self.kv_pe = kv_pe
        self.enc_tok = self.ws.dma(lambda e: e.dma_start(out=self.encb[:], in_=self.enc))
        self.wait_banks("tensor", [4, 5])
        kb.waits("tensor", [self.enc_tok, self.xn_tok[DC - 1]])
        e01_pe = None
        for dc in range(DC):
            lhsT = self.encb[:, dc * 40:(dc + 1) * 40]
            for th in range(2):
                e01_pe = self.mm(self.pCD[0:40, th * 512:(th + 1) * 512], lhsT,
                                 self.xn[:, dc * T + th * 512: dc * T + (th + 1) * 512],
                                 start=(dc == 0), stop=(dc == DC - 1),
                                 sem="pe" if (dc == DC - 1 and th == 1) else None)
        kb.wait("vector", e01_pe)
        e01_tok = kb.op("vector", lambda e: e.tensor_copy(self.e01[0:40, :], self.pCD[0:40, 0:1024]), sem="dve")
        self.e01_tok = e01_tok
        self.set_banks([4, 5], [e01_tok])
        self.p5_loads = []
        for oc in range(3):
            src = self.wq[oc].rearrange("p a b -> p (a b)")
            self.p5_loads.append(self.wst.load(src, split=1))
        kb.wait("vector", kv_pe)
        kb.wait("vector", self.kphi_tok)
        kv_ev = kb.op("vector", lambda e: e.tensor_copy(
            self.msc32[:, 0:1024], self.pAB[:, 0:1024]), sem="dve")
        kb.wait("vector", kv_ev)
        nm_ev = kb.op("vector", lambda e: e.tensor_copy(
            self.msc32[:, 1024:1032], self.pAB[:, 1024:1032]), sem="dve")
        self.set_banks([0, 1], [kv_ev])
        self.set_banks([2], [nm_ev])
        dtok = self.ws.dma(lambda e: e.dma_start(out=self.cc1i[:, :], in_=self.msc32[:, 0:1032]),
                           wars=[kv_ev, nm_ev])
        kb.wait("gpsimd", dtok)
        self.cc1_tok = kb.op("gpsimd", lambda e: e.collective_compute(
            "AllReduce", ALU.add, replica_groups=[[0, 1, 2, 3], [4, 5, 6, 7]],
            ins=[self.cc1i.opt()], outs=[self.cc1o.opt()]), sem="cc")

    def p5_q(self):
        kb = self.kb
        loads = self.p5_loads
        p3w = self.p3_war
        war = {"t": p3w["t"][0] + p3w["t"][1], "e": p3w["e"][0] + p3w["e"][1],
               "u": p3w["u"][0] + p3w["u"][1], "phi": []}
        first = True
        for oc in range(DC):
            slot, dst, toks = loads[oc]
            r = (oc + 2) % 4
            self.wait_banks("tensor", [2 * r, 2 * r + 1])
            if first:
                kb.wait("tensor", self.xn_tok[DC - 1])
                first = False
            reg = self.reg2(r)
            pe_tok = None
            for dc in range(DC):
                if dc == 0:
                    kb.wait("tensor", toks[-1])
                lhsT = dst[:, dc * 128:(dc + 1) * 128]
                for th in range(2):
                    pe_tok = self.mm(reg[:, th * 512:(th + 1) * 512], lhsT,
                                     self.xn[:, dc * T + th * 512: dc * T + (th + 1) * 512],
                                     start=(dc == 0), stop=(dc == DC - 1),
                                     sem="pe" if (dc == DC - 1 and th == 1) else None)
            self.wst.consumed(slot, pe_tok)
            if oc + 3 < DC:
                src = self.wq[oc + 3].rearrange("p a b -> p (a b)")
                loads.append(self.wst.load(src, split=1))
            t = self.msc32[:, 2048:3072]
            ee = self.msc32[:, 3072:4096]
            u = self.scr32[:, 0:1024]
            qdst = self.qa[:, oc * T:(oc + 1) * T]
            nbq = self.bias[:, 32 + oc:33 + oc]
            bqc = self.bias[:, 0 + oc:1 + oc]
            kb.waits("scalar", [pe_tok, self.nbq_tok] + war["t"])
            t_tok = kb.op("scalar", lambda e, o=t, i=reg, b=nbq: e.activation(
                o, i, AF.Relu, bias=b, scale=-1.0), sem="act")
            kb.wait("scalar", t_tok)
            kb.waits("scalar", war["e"])
            e_tok = kb.op("scalar", lambda e, o=ee, i=t: e.activation(o, i, AF.Exp, scale=-1.0), sem="act")
            kb.waits("vector", [t_tok] + war["u"])
            u_tok = kb.op("vector", lambda e, o=u, i=t, b=bqc, p=reg: e.scalar_tensor_tensor(
                o, i, b, p, ALU.add, ALU.add), sem="dve")
            if oc == 0:
                kb.wait("vector", self.kv_pe)
            kb.waits("vector", [u_tok, e_tok] + war["phi"])
            phi_tok = kb.op("vector", lambda e, o=qdst, a=u, b=ee: e.tensor_tensor(
                o, a, b, ALU.add), sem="dve")
            war = {"t": [u_tok, e_tok], "e": [phi_tok], "u": [phi_tok], "phi": []}
            self.set_banks([2 * r, 2 * r + 1], [t_tok, u_tok])
            self.q_tok = phi_tok
            self.q_e_last = e_tok
            if oc == 16:
                self.emit_cc1_readback()

    def emit_cc1_readback(self):
        kb = self.kb
        kb.wait("sync", ("cc", 2))
        rb1 = self.xs.dma(lambda e: e.dma_start(out=self.msc32[:, 0:1032], in_=self.cc1o[:, :]),
                          wars=[self.bk_tok, self.kphi_tok])
        kb.wait("vector", rb1)
        self.kvc_tok = kb.op("vector", lambda e: e.tensor_copy(
            self.kvbf[:, :], self.msc32[:, 0:1024]), sem="dve")
        self.tmp_tok = kb.op("vector", lambda e: e.tensor_scalar_add(
            self.recip[:, 8:16], self.msc32[:, 1024:1032], 1e-10), sem="dve")
        kb.wait("vector", self.tmp_tok)
        self.rc_tok = kb.op("vector", lambda e: e.reciprocal(
            self.recip[:, 0:8], self.recip[:, 8:16]), sem="dve")

    def p6_attn(self):
        kb = self.kb
        kb.waits("tensor", [self.kvc_tok])
        self.attn_tok = None
        at_dve = at_act = None
        for qh in range(DC):
            kvh = qh // 4
            r = (qh + 2) % 4
            self.wait_banks("tensor", [2 * r, 2 * r + 1])
            reg = self.reg2(r)
            lhsT = self.kvbf[:, kvh * 128:(kvh + 1) * 128]
            pe_tok = None
            for th in range(2):
                pe_tok = self.mm(reg[:, th * 512:(th + 1) * 512], lhsT,
                                 self.qa[:, qh * T + th * 512: qh * T + (th + 1) * 512],
                                 start=True, stop=True, sem="pe" if th == 1 else None)
            odst = self.qa[:, qh * T:(qh + 1) * T]
            rcol = self.recip[:, kvh:kvh + 1]
            if qh % 2 == 0:
                kb.waits("vector", [pe_tok, self.rc_tok])
                ts_tok = kb.op("vector", lambda e, o=odst, p=reg, s=rcol:
                               e.tensor_scalar_mul(o, p, s), sem="dve")
                at_dve = ts_tok
            else:
                kb.waits("scalar", [pe_tok, self.rc_tok])
                ts_tok = kb.op("scalar", lambda e, o=odst, p=reg, s=rcol:
                               e.mul(o, p, s), sem="act")
                at_act = ts_tok
            self.set_banks([2 * r, 2 * r + 1], [ts_tok])
        self.attn_tok = at_dve
        self.attn_tok_act = at_act
        w01_cast = []
        for p in range(4):
            slot = self.scr32[0:40, 1024:2048]
            wars = [self.sq_tok[25], self.cast_tok[25]] if p == 0 else [w01_cast[p - 1]]
            dtok = self.xs.dma(lambda e, s=slot, i=self.cc2o[0:40, p * 1024:(p + 1) * 1024]:
                               e.dma_start(out=s, in_=i), wars=wars)
            kb.wait("vector", dtok)
            ct = kb.op("vector", lambda e, o=self.w01e[0:40, p * 1024:(p + 1) * 1024], s=slot:
                       e.tensor_scalar_mul(o, s, 0.5), sem="dve")
            w01_cast.append(ct)
        self.w01_tok = w01_cast[-1]
        self.w01_cast = w01_cast

    def p7_out(self):
        kb = self.kb
        slots = list(self.wring_slots) + [
            self.msc32[:, 0:2048].bitcast(BF16),
            self.msc32[:, 2048:4096].bitcast(BF16),
        ]
        init_wars = list(self.wst.war) + [self.tmp_tok, self.w01_tok]
        wst7 = WStream(kb, self.ws, slots, init_wars)
        loads = []
        for b in range(4):
            oc, w = divmod(b, 2)
            src = self.wbo[oc, w].rearrange("p a b -> p (a b)")
            loads.append(wst7.load(src, split=1))
        stt_tok = [None] * DC
        store_tok = [None] * DC
        for oc in range(DC):
            sb_, sbdst, sbtoks = loads[2 * oc]
            so_, sodst, sotoks = loads[2 * oc + 1]
            r = (oc + 2) % 4
            self.wait_banks("tensor", [2 * r, 2 * r + 1])
            reg = self.reg2(r)
            slot = self.scr32[:, (oc % 2) * 1024:((oc % 2) + 1) * 1024]
            if oc == 0:
                xwars = [self.q_tok]
            elif oc == 1:
                xwars = [self.w01_tok]
            else:
                xwars = [stt_tok[oc - 2]]
            src = self.xt[oc * 128:(oc + 1) * 128, :]
            xdtok = self.xs.dma(lambda e, d=slot, s=src: e.dma_start(out=d, in_=s), wars=xwars)
            pe_tok = None
            for dc in range(DC):
                if dc == 0:
                    kb.wait("tensor", sbtoks[-1])
                lhsT = sbdst[:, dc * 128:(dc + 1) * 128]
                for th in range(2):
                    self.mm(reg[:, th * 512:(th + 1) * 512], lhsT,
                            self.xn[:, dc * T + th * 512: dc * T + (th + 1) * 512],
                            start=(dc == 0), stop=False)
            for vc in range(DC):
                if vc == 0:
                    kb.wait("tensor", sotoks[-1])
                    if oc == 0:
                        kb.waits("tensor", [self.attn_tok, self.attn_tok_act])
                lhsT = sodst[:, vc * 128:(vc + 1) * 128]
                for th in range(2):
                    self.mm(reg[:, th * 512:(th + 1) * 512], lhsT,
                            self.qa[:, vc * T + th * 512: vc * T + (th + 1) * 512],
                            start=False, stop=False)
            if oc == 0:
                kb.waits("tensor", [self.e01_tok, self.w01_tok])
            for th in range(2):
                pe_tok = self.mm(reg[:, th * 512:(th + 1) * 512],
                                 self.w01e[0:40, oc * 128:(oc + 1) * 128],
                                 self.e01[0:40, th * 512:(th + 1) * 512],
                                 start=False, stop=True, sem="pe" if th == 1 else None)
            wst7.consumed(sb_, pe_tok)
            wst7.consumed(so_, pe_tok)
            nb = 2 * oc + 4
            while nb < 2 * DC and nb <= 2 * oc + 5:
                noc, w = divmod(nb, 2)
                src2 = self.wbo[noc, w].rearrange("p a b -> p (a b)")
                loads.append(wst7.load(src2, split=1))
                nb += 1
            stage = self.scr32[:, 2048 + (oc % 2) * 1024:2048 + ((oc % 2) + 1) * 1024]
            bc = self.bias[:, 128 + oc:129 + oc]
            kb.waits("vector", [pe_tok, xdtok, self.bc_tok])
            if oc >= 2:
                kb.wait("vector", store_tok[oc - 2])
            stt_tok[oc] = kb.op("vector", lambda e, o=stage, p=reg, b=bc, x=slot:
                                e.scalar_tensor_tensor(o, p, b, x, ALU.add, ALU.add), sem="dve")
            self.set_banks([2 * r, 2 * r + 1], [stt_tok[oc]])
            dst = self.out[oc * 128:(oc + 1) * 128, :]
            store_tok[oc] = self.os.dma(lambda e, d=dst, s=stage: e.dma_start(out=d, in_=s),
                                        wars=[stt_tok[oc]])
        for sname in ("o0", "o1", "o2", "o3"):
            kb.wait("scalar", (sname, self.kb.cnt[sname]))

    def build(self):
        self.alloc()
        self.p0_setup()
        self.p1_stats()
        self.p3_preload()
        self.p4_w01()
        self.p0_late()
        self.p1_rows()
        self.p2_xn()
        self.p3_kv()
        self.p5_q()
        self.p6_attn()
        self.p7_out()
        kb = self.kb
        with self.st:
            with self.nc.Block() as block:
                @block.tensor
                def _(eng):
                    for f in kb.prog["tensor"]:
                        f(eng)

                @block.vector
                def _(eng):
                    for f in kb.prog["vector"]:
                        f(eng)

                @block.scalar
                def _(eng):
                    for f in kb.prog["scalar"]:
                        f(eng)

                @block.sync
                def _(eng):
                    for f in kb.prog["sync"]:
                        f(eng)

                @block.gpsimd
                def _(eng):
                    for f in kb.prog["gpsimd"]:
                        f(eng)


def build_nc():
    nc = bass.Bass("TRN2", target_bir_lowering=False)
    Builder(nc).build()
    return nc


def _build_F():
    n = D
    t = np.arange(n, dtype=np.float64)[:, None]
    F = np.zeros((n, n), dtype=np.float64)
    F[:, 0] = 1.0 / np.sqrt(n)
    f = np.arange(1, n // 2, dtype=np.float64)[None, :]
    ang = 2.0 * np.pi * f * t / n
    F[:, 1:n - 1:2] = np.sqrt(2.0 / n) * np.cos(ang)
    F[:, 2:n - 1:2] = np.sqrt(2.0 / n) * np.sin(ang)
    F[:, n - 1] = ((-1.0) ** np.arange(n)) / np.sqrt(n)
    return F.astype(np.float32)


def _tile_w(WT, oc_cols):
    O = WT.shape[1]
    return np.ascontiguousarray(
        WT.reshape(32, 128, O // oc_cols, oc_cols).transpose(2, 1, 0, 3))


def kernel(L0_attn_hidden_states, L0_attn_Wq, L0_attn_bq, L0_attn_Wk, L0_attn_bk,
           L0_attn_Wv, L0_attn_bv, L0_attn_Wo, L0_attn_bo, L0_ffn_W_base,
           L0_ffn_b_base, L0_ffn_enc0, L0_ffn_enc1, L0_ffn_coefs0, L0_ffn_coefs1,
           L0_ffn_mask0, L0_ffn_mask1):
    x = np.asarray(L0_attn_hidden_states, np.float32).reshape(NCORES * T, D)

    wq_t = _tile_w(np.ascontiguousarray(np.asarray(L0_attn_Wq, np.float32).T), 128)
    wk_t = _tile_w(np.ascontiguousarray(np.asarray(L0_attn_Wk, np.float32).T), 512)
    wv_t = _tile_w(np.ascontiguousarray(np.asarray(L0_attn_Wv, np.float32).T), 512)
    wb_t = _tile_w(np.ascontiguousarray(np.asarray(L0_ffn_W_base, np.float32).T), 128)
    wo_t = _tile_w(np.ascontiguousarray(np.asarray(L0_attn_Wo, np.float32).T), 128)
    wbo_t = np.ascontiguousarray(np.stack([wb_t, wo_t], axis=1))

    import ml_dtypes
    F = _build_F()
    FT = np.ascontiguousarray(F.T.astype(ml_dtypes.bfloat16))

    enc = np.concatenate([np.asarray(L0_ffn_enc0, np.float32),
                          np.asarray(L0_ffn_enc1, np.float32)], axis=0)
    enc_t = np.ascontiguousarray(enc.T.reshape(32, 128, 40).transpose(1, 0, 2).reshape(128, 1280))

    c01 = np.concatenate([np.asarray(L0_ffn_coefs0, np.float32),
                          np.asarray(L0_ffn_coefs1, np.float32)], axis=2)
    m01 = np.concatenate([np.asarray(L0_ffn_mask0).view(np.uint8),
                          np.asarray(L0_ffn_mask1).view(np.uint8)], axis=2)
    c01_t = c01.reshape(3, 32, 128, 40)
    m01_t = m01.reshape(3, 32, 128, 40)

    bq_c = np.ascontiguousarray(np.asarray(L0_attn_bq, np.float32).reshape(32, 128).T)
    bo_c = np.ascontiguousarray(np.asarray(L0_attn_bo, np.float32).reshape(32, 128).T)
    bb_c = np.ascontiguousarray(np.asarray(L0_ffn_b_base, np.float32).reshape(32, 128).T)
    bk_r = np.ascontiguousarray(np.asarray(L0_attn_bk, np.float32).reshape(1, KVD))
    bv_r = np.ascontiguousarray(np.asarray(L0_attn_bv, np.float32).reshape(1, KVD))

    in_maps = []
    for c in range(NCORES):
        xt_c = np.ascontiguousarray(x[c * T:(c + 1) * T].T)
        ft_c = np.ascontiguousarray(FT[c * 512:(c + 1) * 512, :])
        cc = np.ascontiguousarray(c01_t[:, 4 * c:4 * c + 4].transpose(2, 0, 1, 3).reshape(128, 480))
        mm_ = np.ascontiguousarray(m01_t[:, 4 * c:4 * c + 4].transpose(2, 0, 1, 3).reshape(128, 480))
        in_maps.append({
            "xt": xt_c, "ft": ft_c, "coefs": cc, "masks": mm_,
            "wq": wq_t, "wk": wk_t, "wv": wv_t, "wbo": wbo_t,
            "enc": enc_t, "bq": bq_c, "bo": bo_c, "bb": bb_c,
            "bkr": bk_r, "bvr": bv_r,
        })

    nc = build_nc()
    res = run_bass_kernel_spmd(nc, in_maps, core_ids=list(range(NCORES)))
    outs = np.empty((NCORES * T, D), np.float32)
    for c in range(NCORES):
        outs[c * T:(c + 1) * T] = res.results[c]["out"].T
    return outs.reshape(2, T * 4, D)
